# version 01 (final)
# baseline (speedup 1.0000x reference)
import numpy as np
import ml_dtypes
import concourse.bass as bass
import concourse.mybir as mybir
from concourse.bass_utils import run_bass_kernel_spmd

F32 = mybir.dt.float32
BF16 = mybir.dt.bfloat16
I32 = mybir.dt.int32
AF = mybir.ActivationFunctionType
ALU = mybir.AluOpType
AX = mybir.AxisListType.X

D = 1024
NT = 512
RG_PAIR = [[0, 1], [2, 3], [4, 5], [6, 7]]
RG_ALL = [list(range(8))]


class Sem:
    def __init__(self, nc, name):
        self.h = nc.alloc_semaphore(name)
        self.n = 0

    def inc(self, instr, k=1):
        instr.then_inc(self.h, k)
        self.n += k
        return self.n


class Slab:
    def __init__(self, nc, nbytes):
        self.nc = nc
        self.base_t = nc.alloc_sbuf_tensor("slab", [128, nbytes], mybir.dt.uint8)
        self.base = nc.lookup_mloc(self.base_t).addr
        self.free_list = [(0, nbytes)]
        self.live = {}

    def alloc(self, name, chunks, cols, dtype):
        width = chunks * cols * mybir.dt.size(dtype)
        width = (width + 31) & ~31
        for idx, (off, sz) in enumerate(self.free_list):
            if sz >= width:
                self.free_list[idx] = (off + width, sz - width)
                t = self.nc.alloc_sbuf_tensor_at(
                    name, [128, chunks * cols], dtype, offset=self.base + off)
                self.live[name] = (off, width, t)
                return t
        raise MemoryError(f"slab OOM for {name} ({width}B); free={self.free_list}")

    def alias(self, name, base_name, chunks, cols, dtype):
        off, width, _ = self.live[base_name]
        assert chunks * cols * mybir.dt.size(dtype) <= width
        return self.nc.alloc_sbuf_tensor_at(
            name, [128, chunks * cols], dtype, offset=self.base + off)

    def free(self, *names):
        for name in names:
            off, width, _ = self.live.pop(name)
            self.free_list.append((off, width))
        self.free_list.sort()
        out = []
        for off, sz in self.free_list:
            if out and out[-1][0] + out[-1][1] == off:
                out[-1] = (out[-1][0], out[-1][1] + sz)
            else:
                out.append((off, sz))
        self.free_list = [(o, s) for o, s in out if s > 0]


def build(L=4096, dump=(), upto=None, races=False):
    T = L // 2
    NC8 = L // 128
    NTF = L // NT
    NTO = T // NT
    NCT = T // 128
    HLO = 32
    dump = set(dump)

    nc = bass.Bass("TRN2", target_bir_lowering=False, detect_race_conditions=races)

    dI = {}
    def din(name, shape, dtype=F32):
        dI[name] = nc.dram_tensor(name, shape, dtype, kind="ExternalInput")
        return dI[name]

    din("x_fm", [D, L]); din("x_own", [D, T])
    din("wq_t", [D, 512]); din("wk_t", [D, 512]); din("wv_t", [D, 512])
    din("wb_t", [D, 2])
    din("cw", [128, 48]); din("fir_s", [128, 24]); din("fir_l", [128, 248])
    din("fg1_t", [1792, D]); din("fg1_b", [128, 8])
    din("fg2_t", [D, 4]); din("fg2_b", [4, 1]); din("ltemp", [1, 1])
    din("gam", [128, 8]); din("opj_t", [D, D])
    din("w1_t", [D, 4096]); din("b1c", [128, 32])
    din("w2_t", [4096, D]); din("b2c", [128, 8])
    din("flags", [128, 4]); din("consts", [128, 8]); din("onesf", [1, 128])
    din("identb", [128, 128], BF16); din("identf", [128, 128])
    din("mask_lo", [128, 128], BF16); din("mask_up", [128, 128], BF16)
    din("ones1", [1, 128], BF16); din("onesk", [128, 4])

    out_d = nc.dram_tensor("out", [D, T], F32, kind="ExternalOutput")
    dbg = {}

    vb_d = nc.dram_tensor("vb", [512, L], BF16)
    vag_d = nc.dram_tensor("vag", [1024, L], BF16)
    dlb_d = nc.dram_tensor("dlb", [512, L], BF16)
    dlag_d = nc.dram_tensor("dlag", [1024, L], BF16)
    shb_d = nc.dram_tensor("shb", [1024, T], BF16)
    lgb_d = nc.dram_tensor("lgb", [1024, T], BF16)
    x2b_d = nc.dram_tensor("x2b", [1024, T], BF16)
    hs_d = nc.dram_tensor("hs_d", [1024, L], BF16)
    mx_in = nc.dram_tensor("mx_in", [1, 1], F32)
    mx_out = nc.dram_tensor("mx_out", [1, 1], F32)
    mh_in = nc.dram_tensor("mh_in", [1, 1], F32)
    mh_out = nc.dram_tensor("mh_out", [1, 1], F32)

    pA = nc.alloc_psum_tensor("pA", [128, NT], F32)
    pB = nc.alloc_psum_tensor("pB", [128, NT], F32)
    pC = nc.alloc_psum_tensor("pC", [128, NT], F32)
    pD = nc.alloc_psum_tensor("pD", [128, NT], F32)
    pE = nc.alloc_psum_tensor("pE", [128, NT], F32)
    psm = nc.alloc_psum_tensor("psm", [4, NT], F32)
    pXa = nc.alloc_psum_tensor("pXa", [128, 1024], BF16)
    pXb = nc.alloc_psum_tensor("pXb", [128, 1024], BF16)
    pS = [pA, pB]
    pX = [pXa, pXb]
    pPP = [pD, pE]

    sl = Slab(nc, 202 * 1024)

    cst = sl.alloc("cst", 1, 8, F32)
    flg = sl.alloc("flg", 1, 4, F32)
    onesf = sl.alloc("onesf", 1, 128, F32)
    idb = sl.alloc("idb", 1, 128, BF16)
    idf = sl.alloc("idf", 1, 128, F32)
    mlo = sl.alloc("mlo", 1, 128, BF16)
    mup = sl.alloc("mup", 1, 128, BF16)
    on1 = sl.alloc("on1", 1, 128, BF16)
    onkf = sl.alloc("onkf", 1, 4, F32)
    onk = sl.alloc("onk", 1, 4, BF16)
    cwb = sl.alloc("cwb", 1, 48, F32)
    firs = sl.alloc("firs", 1, 24, F32)
    firl = sl.alloc("firl", 1, 248, F32)
    gamc = sl.alloc("gamc", 1, 8, F32)
    fgb1 = sl.alloc("fgb1", 1, 8, F32)
    fgb2 = sl.alloc("fgb2", 1, 1, F32)
    ltmp = sl.alloc("ltmp", 1, 1, F32)
    b1cs = sl.alloc("b1cs", 1, 32, F32)
    b2cs = sl.alloc("b2cs", 1, 8, F32)
    btm = sl.alloc("btm", 1, 2 * NC8, F32)
    nbtm = sl.alloc("nbtm", 1, 2 * NC8, F32)
    scr128 = sl.alloc("scr128", 1, 128, F32)
    rbl2 = sl.alloc("rbl2", 1, L, BF16)
    scl = sl.alloc("scl", 1, 16, F32)
    sclc = sl.alloc("sclc", 1, 8, F32)
    b1s = sl.alloc("b1s", 1, 32, F32)
    b2s = sl.alloc("b2s", 1, 8, F32)
    wmc = sl.alloc("wmc", 1, 8, F32)
    mxc = sl.alloc("mxc", 1, 64, F32)
    wmxa = sl.alloc("wmxa", 1, 64, F32)

    sD = Sem(nc, "sD"); sG = Sem(nc, "sG"); sP = Sem(nc, "sP")
    sW = Sem(nc, "sW")
    sE = [Sem(nc, "sE0"), Sem(nc, "sE1")]
    sF = [Sem(nc, "sF0"), Sem(nc, "sF1")]
    sOp = [Sem(nc, "sOp0"), Sem(nc, "sOp1")]
    sCC = Sem(nc, "sCC"); sT = Sem(nc, "sT"); sV = Sem(nc, "sV")
    sS = Sem(nc, "sS"); sO = Sem(nc, "sO")

    EPS_LN = cst[0:1, 0:1]; EPS_L2 = cst[0:1, 2:3]; EPS_RMS = cst[0:1, 3:4]
    F_COL = flg[:, 0:1]; G_COL = flg[:, 1:2]; C127 = flg[0:1, 2:3]


    def ch(t, c, cols, j0=0, n=None):
        if n is None:
            n = cols - j0
        return t[:, c * cols + j0: c * cols + j0 + n]

    def dump_bf16(blk, name, t, chunks, cols):
        if name not in dump:
            return
        dbg[name] = nc.dram_tensor("dbg_" + name, [chunks * 128, cols], F32,
                                   kind="ExternalOutput")
        def f(e, name=name, t=t, chunks=chunks, cols=cols):
            for c in range(chunks):
                sG.inc(e.dma_start(out=dbg[name][c * 128:(c + 1) * 128, :],
                                   in_=ch(t, c, cols)), 16)
            e.wait_ge(sG.h, sG.n)
        blk.gpsimd(f)
        gn = sG.n
        for engf in (blk.vector, blk.scalar, blk.tensor, blk.sync):
            engf(lambda e, gn=gn: e.wait_ge(sG.h, gn))

    def dump_row(blk, name, ap, rows, cols):
        if name not in dump:
            return
        dbg[name] = nc.dram_tensor("dbg_" + name, [rows, cols], F32,
                                   kind="ExternalOutput")
        def f(e, name=name, ap=ap):
            sG.inc(e.dma_start(out=dbg[name][:, :], in_=ap), 16)
            e.wait_ge(sG.h, sG.n)
        blk.gpsimd(f)
        gn = sG.n
        for engf in (blk.vector, blk.scalar, blk.tensor, blk.sync):
            engf(lambda e, gn=gn: e.wait_ge(sG.h, gn))


    def vchain(e, *fns):
        for fn in fns:
            ins = fn(e)
            sV.inc(ins)
            e.wait_ge(sV.h, sV.n)
    with nc.Block() as blk:
        def f(e):
            for t, d in [(cst, "consts"), (flg, "flags"), (idf, "identf"),
                         (cwb, "cw"), (firs, "fir_s"), (firl, "fir_l"),
                         (gamc, "gam"), (fgb1, "fg1_b"),
                         (b1cs, "b1c"), (b2cs, "b2c"), (onkf, "onesk")]:
                sD.inc(e.dma_start(out=t[:, :], in_=dI[d][:, :]), 16)
            sD.inc(e.dma_start(out=ltmp[0:1, :], in_=dI["ltemp"][:, :]), 16)
            sD.inc(e.dma_start(out=fgb2[0:4, :], in_=dI["fg2_b"][:, :]), 16)
            sD.inc(e.dma_start(out=onesf[0:1, :], in_=dI["onesf"][:, :]), 16)
        blk.sync(f)
        def f(e):
            for t, d in [(idb, "identb"), (mlo, "mask_lo"), (mup, "mask_up")]:
                sG.inc(e.dma_start(out=t[:, :], in_=dI[d][:, :]), 16)
            sG.inc(e.dma_start(out=on1[0:1, :], in_=dI["ones1"][:, :]), 16)
        blk.gpsimd(f)
        def f(e):
            e.wait_ge(sD.h, sD.n)
            e.wait_ge(sG.h, sG.n)
            e.tensor_copy(onk[:, :], onkf[:, :])
        blk.vector(f)

    sqb = [sl.alloc("sqb0", 1, NT, BF16), sl.alloc("sqb1", 1, NT, BF16)]
    _ln_tag = [0]

    def ln_pass(src, chunks, cols, ntiles, out_t, dram_out=None):
        Lx = ntiles * NT
        tg = _ln_tag[0]; _ln_tag[0] += 1
        srow = sl.alloc(f"srow{tg}", 1, 3 * Lx, F32)
        brow = sl.alloc(f"brow{tg}", 1, 2 * Lx, BF16)
        bc0 = sl.alloc(f"bc0_{tg}", 1, Lx, BF16)
        bc1 = sl.alloc(f"bc1_{tg}", 1, Lx, BF16)
        MEAN = srow[0:1, 0:Lx]; MSQ = srow[0:1, Lx:2 * Lx]
        RROW = srow[0:1, 2 * Lx:3 * Lx]; MRROW = srow[0:1, Lx:2 * Lx]
        RB = brow[0:1, 0:Lx]; MRB = brow[0:1, Lx:2 * Lx]
        with nc.Block() as blk:
            tmark = {}
            for nt in range(ntiles):
                for c in range(chunks):
                    k = nt * chunks + c
                    bi = k % 2
                    def f(e, c=c, nt=nt, bi=bi, k=k):
                        if k >= 2:
                            e.wait_ge(sT.h, tmark[k - 2])
                        _last = e.activation(sqb[bi][:, :], ch(src, c, cols, nt * NT, NT), AF.Square)
                        sS.inc(_last)
                    blk.scalar(f)
                    aq = sS.n
                    def f(e, c=c, nt=nt, bi=bi, aq=aq, k=k):
                        e.matmul(psm[0:1, 0:NT], onk[:, 0:1], ch(src, c, cols, nt * NT, NT),
                                 start=(c == 0), stop=(c == chunks - 1))
                        e.wait_ge(sS.h, aq)
                        ins = e.matmul(pC[0:1, 0:NT], onk[:, 0:1], sqb[bi][:, :],
                                       start=(c == 0), stop=(c == chunks - 1))
                        tmark[k] = sT.inc(ins)
                    blk.tensor(f)
                tm = sT.n
                def f(e, nt=nt, tm=tm):
                    e.wait_ge(sT.h, tm)
                    e.tensor_copy(MEAN[:, nt * NT:(nt + 1) * NT], psm[0:1, 0:NT])
                    _last = e.tensor_copy(MSQ[:, nt * NT:(nt + 1) * NT], pC[0:1, 0:NT])
                    sV.inc(_last)
                blk.vector(f)
                vv = sV.n
                if nt + 1 < ntiles:
                    def f(e, vv=vv):
                        e.wait_ge(sV.h, vv)
                        e.nop()
                    blk.tensor(f)
            vs = sV.n
            def f(e, vs=vs, Lx=Lx):
                e.wait_ge(sV.h, vs)
                e.tensor_tensor(RROW[:, 0:Lx], MEAN[:, 0:Lx], MEAN[:, 0:Lx], ALU.mult)
                _last = e.tensor_sub(RROW[:, 0:Lx], MSQ[:, 0:Lx], RROW[:, 0:Lx])
                sV.inc(_last)
            blk.vector(f)
            vv2 = sV.n
            def f(e, vv2=vv2, Lx=Lx):
                e.wait_ge(sV.h, vv2)
                _last = e.activation(RROW[:, 0:Lx], RROW[:, 0:Lx], AF.Sqrt, bias=EPS_LN)
                sS.inc(_last)
            blk.scalar(f)
            av = sS.n
            def f(e, av=av, Lx=Lx):
                e.wait_ge(sS.h, av)
                e.reciprocal(RROW[:, 0:Lx], RROW[:, 0:Lx])
                e.tensor_tensor(MRROW[:, 0:Lx], MEAN[:, 0:Lx], RROW[:, 0:Lx], ALU.mult)
                e.tensor_copy(RB[:, 0:Lx], RROW[:, 0:Lx])
                _last = e.tensor_copy(MRB[:, 0:Lx], MRROW[:, 0:Lx])
                sV.inc(_last)
            blk.vector(f)
            vr = sV.n
            smark = {}
            for nt in range(ntiles):
                def f(e, nt=nt, vr=vr):
                    if nt == 0:
                        e.wait_ge(sV.h, vr)
                    if nt >= 1:
                        e.wait_ge(sS.h, smark[nt - 1])
                    e.matmul(pD[:, 0:NT], on1[0:1, :], RB[0:1, nt * NT:(nt + 1) * NT])
                    ins = e.matmul(pE[:, 0:NT], on1[0:1, :], MRB[0:1, nt * NT:(nt + 1) * NT])
                    sT.inc(ins)
                blk.tensor(f)
                tb = sT.n
                def f(e, nt=nt, tb=tb):
                    e.wait_ge(sT.h, tb)
                    e.copy(bc0[:, nt * NT:(nt + 1) * NT], pD[:, 0:NT])
                    _last = e.copy(bc1[:, nt * NT:(nt + 1) * NT], pE[:, 0:NT])
                    smark[nt] = sS.inc(_last)
                blk.scalar(f)
            ab = sS.n
            if dram_out is None:
                def f(e, ab=ab, Lx=Lx):
                    e.wait_ge(sS.h, ab)
                    for c in range(chunks):
                        e.tensor_tensor(ch(out_t, c, cols), ch(src, c, cols),
                                        bc0[:, 0:Lx], ALU.mult)
                        _last = e.tensor_sub(ch(out_t, c, cols), ch(out_t, c, cols),
                                             bc1[:, 0:Lx])
                    sV.inc(_last)
                blk.vector(f)
            else:
                hmark = {}
                for c in range(chunks):
                    def f(e, c=c, ab=ab, Lx=Lx):
                        if c == 0:
                            e.wait_ge(sS.h, ab)
                        if c >= 2:
                            e.wait_ge(sD.h, hmark[c - 2])
                        e.tensor_tensor(ch(out_t, c % 2, cols), ch(src, c, cols),
                                        bc0[:, 0:Lx], ALU.mult)
                        _last = e.tensor_sub(ch(out_t, c % 2, cols),
                                             ch(out_t, c % 2, cols), bc1[:, 0:Lx])
                        sV.inc(_last)
                    blk.vector(f)
                    vc_ = sV.n
                    def f(e, c=c, vc_=vc_, cols=cols):
                        e.wait_ge(sV.h, vc_)
                        ins = e.dma_start(out=dram_out[c * 128:(c + 1) * 128, :],
                                          in_=ch(out_t, c % 2, cols))
                        hmark[c] = sD.inc(ins, 16)
                    blk.sync(f)
                def f(e):
                    e.wait_ge(sD.h, sD.n)
                blk.sync(f)
        sl.free(f"srow{tg}", f"brow{tg}", f"bc0_{tg}", f"bc1_{tg}")

    xb = sl.alloc("xb", 8, L, BF16)
    hs2t = sl.alloc("hs2t", 2, L, BF16)
    with nc.Block() as blk:
        def f(e):
            for c in range(8):
                sG.inc(e.dma_start(out=ch(xb, c, L),
                                   in_=dI["x_fm"][c * 128:(c + 1) * 128, :]), 16)
            e.wait_ge(sG.h, sG.n)
        blk.gpsimd(f)
    ln_pass(xb, 8, L, NTF, hs2t, dram_out=hs_d)
    sl.free("xb", "hs2t")
    if upto is not None and upto <= 10:
        return nc, sorted(dbg.keys())

    wp = sl.alloc("wp", 1, 8 * NT, BF16)
    qn = sl.alloc("qn", 4, L, BF16)
    kn = sl.alloc("kn", 4, L, BF16)
    v_own = sl.alloc("v_own", 4, L, BF16)
    rawp = sl.alloc("rawp", 4, L, BF16)
    accv = sl.alloc("accv", 1, L, F32)
    ssq2 = accv
    bcn = sl.alloc("bcn", 1, L, BF16)
    hsb = [sl.alloc("hsb0", 8, NT, BF16), sl.alloc("hsb1", 8, NT, BF16)]
    wbet = sl.alloc("wbet", 1, 16, BF16)
    bsb = sl.alloc("bsb", 1, L, BF16)

    def emit_proj(wname, p_idx, dest, do_l2n, with_beta=False):
        with nc.Block() as blk:
            def f(e, wname=wname):
                for kc in range(8):
                    sG.inc(e.dma_start(out=wp[:, kc * NT:(kc + 1) * NT],
                                       in_=dI[wname][kc * 128:(kc + 1) * 128, :]), 16)
                if with_beta:
                    sG.inc(e.dma_start(
                        out=wbet[:, :].rearrange("p (kc m) -> p kc m", m=2),
                        in_=dI["wb_t"][:, :].rearrange("(kc p) m -> p kc m", p=128)), 16)
            blk.gpsimd(f)
            gW = sG.n
            smark = {}
            bmark = {}
            for nt in range(NTF):
                def f(e, nt=nt, gW=gW):
                    if nt >= 2:
                        e.wait_ge(sT.h, smark[("t", nt - 2)])
                    ins = e.dma_start(
                        out=hsb[nt % 2][:, :].rearrange("p (c n) -> p c n", n=NT),
                        in_=hs_d[:, nt * NT:(nt + 1) * NT].rearrange(
                            "(c p) n -> p c n", p=128))
                    smark[("h", nt)] = sF[nt % 2].inc(ins, 16)
                blk.sync(f)
                for mc in range(4):
                    idx = nt * 4 + mc
                    def f(e, nt=nt, mc=mc, idx=idx, gW=gW):
                        if idx == 0:
                            e.wait_ge(sG.h, gW)
                        e.wait_ge(sF[nt % 2].h, smark[("h", nt)])
                        if idx >= 2:
                            e.wait_ge(sS.h, smark[idx - 2])
                        pp = pPP[idx % 2]
                        for kc in range(8):
                            ins = e.matmul(pp[:, 0:NT],
                                           wp[:, kc * NT + mc * 128: kc * NT + mc * 128 + 128],
                                           ch(hsb[nt % 2], kc, NT),
                                           start=(kc == 0), stop=(kc == 7))
                        smark[("t", nt)] = sT.inc(ins)
                    blk.tensor(f)
                    tM = sT.n
                    def f(e, mc=mc, nt=nt, idx=idx, tM=tM):
                        e.wait_ge(sT.h, tM)
                        _last = e.copy(ch(rawp, mc, L, nt * NT, NT), pPP[idx % 2][:, 0:NT])
                        smark[idx] = sS.inc(_last)
                    blk.scalar(f)
                if with_beta:
                    def f(e, nt=nt):
                        if nt >= 1:
                            e.wait_ge(sS.h, bmark[nt - 1])
                        for kc in range(8):
                            ins = e.matmul(psm[0:2, 0:NT], wbet[:, kc * 2:(kc + 1) * 2],
                                           ch(hsb[nt % 2], kc, NT),
                                           start=(kc == 0), stop=(kc == 7))
                        smark[("t", nt)] = sT.inc(ins)
                    blk.tensor(f)
                    tb = sT.n
                    def f(e, nt=nt, tb=tb):
                        e.wait_ge(sT.h, tb)
                        _last = e.activation(bsb[0:2, nt * NT:(nt + 1) * NT],
                                             psm[0:2, 0:NT], AF.Sigmoid)
                        bmark[nt] = sS.inc(_last)
                    blk.scalar(f)
            for mc in range(4):
                aR = sS.n
                def f(e, mc=mc, aR=aR, p_idx=p_idx):
                    e.wait_ge(sS.h, aR)
                    wc = lambda j: cwb[:, p_idx * 16 + mc * 4 + j: p_idx * 16 + mc * 4 + j + 1]
                    e.tensor_scalar(accv[:, 0:L], ch(rawp, mc, L), wc(3), None, ALU.mult)
                    for j in (2, 1, 0):
                        dd = 3 - j
                        _last = e.scalar_tensor_tensor(accv[:, dd:L], ch(rawp, mc, L, 0, L - dd),
                                               wc(j), accv[:, dd:L], ALU.mult, ALU.add)
                    sV.inc(_last)
                blk.vector(f)
                vC = sV.n
                def f(e, mc=mc, vC=vC, dest=dest):
                    e.wait_ge(sV.h, vC)
                    _last = e.activation(ch(dest, mc, L), accv[:, 0:L], AF.Silu)
                    sS.inc(_last)
                blk.scalar(f)
        if not do_l2n:
            return
        with nc.Block() as blk:
            for h in range(2):
                tmark = {}
                for nt in range(NTF):
                    for ci in range(2):
                        mc = 2 * h + ci
                        k = nt * 2 + ci
                        def f(e, mc=mc, ci=ci, nt=nt, k=k, dest=dest):
                            if k >= 2:
                                e.wait_ge(sT.h, tmark[k - 2])
                            _last = e.activation(sqb[ci][:, :], ch(dest, mc, L, nt * NT, NT),
                                         AF.Square)
                            sS.inc(_last)
                        blk.scalar(f)
                    aq = sS.n
                    def f(e, nt=nt, aq=aq, h=h):
                        e.wait_ge(sS.h, aq)
                        if nt >= 1:
                            e.wait_ge(sV.h, sV.n)
                        for ci in range(2):
                            ins = e.matmul(psm[0:1, 0:NT], onk[:, 2:3], sqb[ci][:, :],
                                           start=(ci == 0), stop=(ci == 1))
                        tmark[nt * 2] = tmark[nt * 2 + 1] = sT.inc(ins)
                    blk.tensor(f)
                    tq = sT.n
                    def f(e, nt=nt, tq=tq):
                        e.wait_ge(sT.h, tq)
                        _last = e.tensor_copy(ssq2[0:1, nt * NT:(nt + 1) * NT], psm[0:1, 0:NT])
                        sV.inc(_last)
                    blk.vector(f)
                vq = sV.n
                def f(e, vq=vq):
                    e.wait_ge(sV.h, vq)
                    _last = e.activation(ssq2[0:1, 0:L], ssq2[0:1, 0:L], AF.Sqrt, bias=EPS_L2)
                    sS.inc(_last)
                blk.scalar(f)
                ar = sS.n
                def f(e, ar=ar):
                    e.wait_ge(sS.h, ar)
                    e.reciprocal(ssq2[0:1, 0:L], ssq2[0:1, 0:L])
                    _last = e.tensor_copy(rbl2[0:1, 0:L], ssq2[0:1, 0:L])
                    sV.inc(_last)
                blk.vector(f)
                vr = sV.n
                smark = {}
                for nt in range(NTF):
                    def f(e, nt=nt, vr=vr):
                        if nt == 0:
                            e.wait_ge(sV.h, vr)
                        if nt >= 2:
                            e.wait_ge(sS.h, smark[nt - 2])
                        ins = e.matmul(pPP[nt % 2][:, 0:NT], on1[0:1, :],
                                       rbl2[0:1, nt * NT:(nt + 1) * NT])
                        sT.inc(ins)
                    blk.tensor(f)
                    tb = sT.n
                    def f(e, nt=nt, tb=tb):
                        e.wait_ge(sT.h, tb)
                        _last = e.copy(bcn[:, nt * NT:(nt + 1) * NT], pPP[nt % 2][:, 0:NT])
                        smark[nt] = sS.inc(_last)
                    blk.scalar(f)
                ab = sS.n
                def f(e, h=h, ab=ab, dest=dest):
                    e.wait_ge(sS.h, ab)
                    for mc in (2 * h, 2 * h + 1):
                        _last = e.tensor_tensor(ch(dest, mc, L), ch(dest, mc, L),
                                        bcn[:, 0:L], ALU.mult)
                    sV.inc(_last)
                blk.vector(f)

    emit_proj("wq_t", 0, qn, True)
    with nc.Block() as blk:
        dump_bf16(blk, "qn", qn, 4, L)
    if upto is not None and upto <= 12:
        return nc, sorted(dbg.keys())
    emit_proj("wk_t", 1, kn, True)
    emit_proj("wv_t", 2, v_own, False, with_beta=True)
    with nc.Block() as blk:
        dump_bf16(blk, "kn", kn, 4, L)
        dump_bf16(blk, "v_own", v_own, 4, L)
    if upto is not None and upto <= 14:
        return nc, sorted(dbg.keys())

    with nc.Block() as blk:
        vmark = {}
        for i in range(NC8):
            def f(e, i=i):
                if i >= 2:
                    e.wait_ge(sV.h, vmark[i - 2])
                ins = e.transpose(pXa[:, i % 2 * 2:i % 2 * 2 + 2],
                                  bsb[0:2, i * 128:(i + 1) * 128], idb[0:2, 0:2])
                sT.inc(ins)
            blk.tensor(f)
            ti = sT.n
            def f(e, i=i, ti=ti):
                e.wait_ge(sT.h, ti)
                _last = e.tensor_copy(btm[:, i * 2:(i + 1) * 2], pXa[:, i % 2 * 2:i % 2 * 2 + 2])
                vmark[i] = sV.inc(_last)
            blk.vector(f)
        def f(e):
            e.wait_ge(sV.h, sV.n)
            _last = e.tensor_scalar(nbtm[:, :], btm[:, :], -1.0, None, ALU.mult)
            sV.inc(_last)
        blk.vector(f)
    with nc.Block() as blk:
        dump_row(blk, "beta", bsb[0:2, 0:L], 2, L)
    if upto is not None and upto <= 16:
        return nc, sorted(dbg.keys())

    sl.free("rawp", "accv", "wp", "wbet", "bsb", "bcn", "hsb0", "hsb1")

    with nc.Block() as blk:
        def f(e):
            for mc in range(4):
                sD.inc(e.dma_start(out=vb_d[mc * 128:(mc + 1) * 128, :],
                                   in_=ch(v_own, mc, L)), 16)
        blk.sync(f)
        dV = sD.n
        def f(e, dV=dV):
            e.wait_ge(sD.h, dV)
            ins = e.collective_compute("AllGather", ALU.bypass, replica_groups=RG_PAIR,
                                       ins=[vb_d.ap()], outs=[vag_d.ap()])
            sCC.inc(ins)
        blk.gpsimd(f)
    cc_x1 = sCC.n
    if upto is not None and upto <= 30:
        return nc, sorted(dbg.keys())

    delta = sl.alloc("delta", 4, L, BF16)
    scr = {}
    for h in range(2):
        scr[h] = dict(
            B0=sl.alloc(f"B0_{h}", 1, 128, BF16), B1=sl.alloc(f"B1_{h}", 1, 128, BF16),
            BT0=sl.alloc(f"BT0_{h}", 1, 128, BF16), BT1=sl.alloc(f"BT1_{h}", 1, 128, BF16),
            PT=sl.alloc(f"PT_{h}", 1, 128, BF16), AT=sl.alloc(f"AT_{h}", 1, 128, BF16),
            VT=sl.alloc(f"VT_{h}", 1, 256, BF16), KT=sl.alloc(f"KT_{h}", 1, 256, BF16),
            Y=sl.alloc(f"Y_{h}", 1, 256, BF16), YF=sl.alloc(f"YF_{h}", 1, 256, F32),
            U=sl.alloc(f"U_{h}", 1, 256, BF16), SS=sl.alloc(f"SS_{h}", 1, 512, BF16),
        )
    wmt = [sl.alloc("wmt0", 1, 2048, F32), sl.alloc("wmt1", 1, 2048, F32)]

    wjobs = []
    for w_i, wd in enumerate(["w1_t", "w2_t"]):
        nrows, ncols = dI[wd].shape
        cbs = min(ncols, 2048)
        for r in range(0, nrows, 128):
            for cb in range(ncols // cbs):
                wjobs.append((w_i, wd, r, cb, cbs))
    wstate = {"i": 0, "marks": {}, "first": [True, True]}

    def emit_wjob(blk):
        if wstate["i"] >= len(wjobs):
            return
        j = wstate["i"]
        w_i, wd, r, cb, cbs = wjobs[j]
        def f(e, j=j, wd=wd, r=r, cb=cb, cbs=cbs):
            if j >= 2:
                e.wait_ge(sV.h, wstate["marks"][("v", j - 2)])
            ins = e.dma_start(out=wmt[j % 2][:, 0:cbs],
                              in_=dI[wd][r:r + 128, cb * cbs:(cb + 1) * cbs])
            wstate["marks"][("d", j)] = sE[j % 2].inc(ins, 16)
        blk.sync(f)
        def f(e, j=j, cbs=cbs):
            e.wait_ge(sE[j % 2].h, wstate["marks"][("d", j)])
            _last = e.reduce_max(wmxa[:, j:j + 1], wmt[j % 2][:, 0:cbs], axis=AX,
                                 apply_absolute_value=True)
            wstate["marks"][("v", j)] = sV.inc(_last)
        blk.vector(f)
        wstate["i"] += 1

    pre_done = {}
    scan_done = {}
    chain_last_v = [0]

    with nc.Block() as blk:
        def emit_pre(i, h):
            s = scr[h]
            def f(e, i=i, h=h):
                e.wait_ge(sV.h, chain_last_v[0])
                for kc in range(2):
                    e.matmul(pC[:, 0:128],
                             ch(kn, 2 * h + kc, L, i * 128, 128),
                             ch(kn, 2 * h + kc, L, i * 128, 128),
                             start=(kc == 0), stop=(kc == 1))
                for kc in range(2):
                    ins = e.matmul(pC[:, 128:256],
                                   ch(kn, 2 * h + kc, L, i * 128, 128),
                                   ch(qn, 2 * h + kc, L, i * 128, 128),
                                   start=(kc == 0), stop=(kc == 1))
                sT.inc(ins)
            blk.tensor(f)
            tg = sT.n
            def f(e, i=i, h=h, s=s, tg=tg):
                e.wait_ge(sT.h, tg)
                nb = nbtm[:, i * 2 + h: i * 2 + h + 1]
                e.scalar_tensor_tensor(s["B0"][:, :], pC[:, 0:128], nb, mlo[:, :],
                                       ALU.mult, ALU.mult)
                _last = e.tensor_tensor(s["AT"][:, :], pC[:, 128:256], mup[:, :], ALU.mult)
                sV.inc(_last)
            blk.vector(f)
            vbm = sV.n
            def f(e, h=h, s=s, vbm=vbm):
                e.wait_ge(sV.h, vbm)
                ins = e.transpose(pX[h][:, 0:128], s["B0"][:, :], idb[:, :])
                sT.inc(ins)
            blk.tensor(f)
            tbt = sT.n
            def f(e, h=h, s=s, tbt=tbt):
                e.wait_ge(sT.h, tbt)
                e.tensor_add(s["PT"][:, :], pX[h][:, 0:128], idb[:, :])
                _last = e.tensor_copy(s["BT0"][:, :], pX[h][:, 0:128])
                sV.inc(_last)
            blk.vector(f)
            last_v = sV.n
            cb, cb2, tb, tb2 = "B0", "B1", "BT0", "BT1"
            for j in range(6):
                def f(e, s=s, cb=cb, tb=tb, last_v=last_v):
                    e.wait_ge(sV.h, last_v)
                    e.matmul(pC[:, 0:128], s[tb][:, :], s[cb][:, :])
                    ins = e.matmul(pC[:, 128:256], s[cb][:, :], s[tb][:, :])
                    sT.inc(ins)
                blk.tensor(f)
                tsq = sT.n
                def f(e, s=s, cb2=cb2, tb2=tb2, tsq=tsq):
                    e.wait_ge(sT.h, tsq)
                    e.copy(s[cb2][:, :], pC[:, 0:128])
                    _last = e.copy(s[tb2][:, :], pC[:, 128:256])
                    sS.inc(_last)
                blk.scalar(f)
                aev = sS.n
                def f(e, s=s, cb2=cb2, aev=aev):
                    e.wait_ge(sS.h, aev)
                    ins = e.matmul(pC[:, 256:384], s[cb2][:, :], s["PT"][:, :])
                    sT.inc(ins)
                blk.tensor(f)
                tpd = sT.n
                def f(e, s=s, tpd=tpd):
                    e.wait_ge(sT.h, tpd)
                    _last = e.tensor_add(s["PT"][:, :], pC[:, 256:384], s["PT"][:, :])
                    sV.inc(_last)
                blk.vector(f)
                last_v = sV.n
                cb, cb2, tb, tb2 = cb2, cb, tb2, tb
            chain_last_v[0] = last_v
            pre_done[(i, h)] = last_v

        def emit_scan(i, h):
            s = scr[h]
            def f(e, i=i, h=h, s=s):
                e.wait_ge(sV.h, pre_done[(i, h)])
                if i > 0:
                    e.wait_ge(sS.h, scan_done[(i - 1, h)])
                    for kc in range(2):
                        e.matmul(pD[:, 0:256],
                                 ch(kn, 2 * h + kc, L, i * 128, 128),
                                 s["SS"][:, kc * 256:(kc + 1) * 256],
                                 start=(kc == 0), stop=(kc == 1))
                for kc in range(2):
                    ins = e.transpose(pX[h][:, 256 + kc * 128: 256 + (kc + 1) * 128],
                                      ch(v_own, 2 * h + kc, L, i * 128, 128), idb[:, :])
                sT.inc(ins)
            blk.tensor(f)
            tz = sT.n
            def f(e, h=h, s=s, tz=tz):
                e.wait_ge(sT.h, tz)
                _last = e.copy(s["VT"][:, :], pX[h][:, 256:512])
                sS.inc(_last)
            blk.scalar(f)
            av = sS.n
            def f(e, i=i, h=h, s=s, av=av):
                e.wait_ge(sS.h, av)
                bcol = btm[:, i * 2 + h: i * 2 + h + 1]
                if i > 0:
                    e.tensor_sub(s["YF"][:, :], s["VT"][:, :], pD[:, 0:256])
                    _last = e.tensor_scalar(s["Y"][:, :], s["YF"][:, :], bcol, None, ALU.mult)
                else:
                    _last = e.tensor_scalar(s["Y"][:, :], s["VT"][:, :], bcol, None, ALU.mult)
                sV.inc(_last)
            blk.vector(f)
            vy = sV.n
            def f(e, s=s, vy=vy):
                e.wait_ge(sV.h, vy)
                ins = e.matmul(pD[:, 256:512], s["PT"][:, :], s["Y"][:, :])
                sT.inc(ins)
            blk.tensor(f)
            tu = sT.n
            def f(e, s=s, tu=tu):
                e.wait_ge(sT.h, tu)
                _last = e.copy(s["U"][:, :], pD[:, 256:512])
                sS.inc(_last)
            blk.scalar(f)
            au = sS.n
            def f(e, i=i, h=h, s=s, au=au):
                e.wait_ge(sS.h, au)
                for dvc in range(2):
                    if i > 0:
                        for kc in range(2):
                            e.matmul(pE[:, dvc * 128:(dvc + 1) * 128],
                                     s["SS"][:, kc * 256 + dvc * 128:
                                             kc * 256 + (dvc + 1) * 128],
                                     ch(qn, 2 * h + kc, L, i * 128, 128),
                                     start=(kc == 0), stop=False)
                        e.matmul(pE[:, dvc * 128:(dvc + 1) * 128],
                                 s["U"][:, dvc * 128:(dvc + 1) * 128], s["AT"][:, :],
                                 start=False, stop=True)
                    else:
                        e.matmul(pE[:, dvc * 128:(dvc + 1) * 128],
                                 s["U"][:, dvc * 128:(dvc + 1) * 128], s["AT"][:, :],
                                 start=True, stop=True)
                for kc in range(2):
                    ins = e.transpose(pX[h][:, 256 + kc * 128: 256 + (kc + 1) * 128],
                                      ch(kn, 2 * h + kc, L, i * 128, 128), idb[:, :])
                sT.inc(ins)
            blk.tensor(f)
            to = sT.n
            def f(e, i=i, h=h, s=s, to=to):
                e.wait_ge(sT.h, to)
                for dvc in range(2):
                    e.copy(ch(delta, h * 2 + dvc, L, i * 128, 128),
                           pE[:, dvc * 128:(dvc + 1) * 128])
                _last = e.copy(s["KT"][:, :], pX[h][:, 256:512])
                sS.inc(_last)
            blk.scalar(f)
            ak = sS.n
            def f(e, i=i, h=h, s=s, ak=ak):
                e.wait_ge(sS.h, ak)
                for kc in range(2):
                    ins = e.matmul(pS[h][:, kc * 256:(kc + 1) * 256],
                                   s["KT"][:, kc * 128:(kc + 1) * 128], s["U"][:, :],
                                   start=(i == 0 and kc == 0), stop=(i == NC8 - 1),
                                   skip_group_check=True)
                sT.inc(ins)
            blk.tensor(f)
            tSu = sT.n
            def f(e, i=i, h=h, s=s, tSu=tSu):
                e.wait_ge(sT.h, tSu)
                _last = e.copy(s["SS"][:, :], pS[h][:, :])
                scan_done[(i, h)] = sS.inc(_last)
            blk.scalar(f)

        emit_pre(0, 0)
        emit_pre(0, 1)
        for i in range(NC8):
            emit_scan(i, 0)
            if i + 1 < NC8:
                emit_pre(i + 1, 0)
            emit_wjob(blk)
            emit_scan(i, 1)
            if i + 1 < NC8:
                emit_pre(i + 1, 1)
            emit_wjob(blk)
        while wstate["i"] < len(wjobs):
            emit_wjob(blk)
        NW1 = sum(1 for (w_i, *_r) in wjobs if w_i == 0)
        def f(e, NW1=NW1, NW=len(wjobs)):
            e.wait_ge(sV.h, wstate["marks"][("v", NW - 1)])
            _last = e.reduce_max(wmc[:, 0:1], wmxa[:, 0:NW1], axis=AX)
            sV.inc(_last)
            e.wait_ge(sV.h, sV.n)
            _last = e.reduce_max(wmc[:, 1:2], wmxa[:, NW1:NW], axis=AX)
            sV.inc(_last)
        blk.vector(f)

    with nc.Block() as blk:
        dump_bf16(blk, "delta", delta, 4, L)
    for h in range(2):
        sl.free(*[f"{k}_{h}" for k in
                  ("B0", "B1", "BT0", "BT1", "PT", "AT", "VT", "KT", "Y", "YF", "U", "SS")])
    sl.free("wmt0", "wmt1")
    if upto is not None and upto <= 40:
        return nc, sorted(dbg.keys())

    with nc.Block() as blk:
        def f(e):
            for mc in range(4):
                sD.inc(e.dma_start(out=dlb_d[mc * 128:(mc + 1) * 128, :],
                                   in_=ch(delta, mc, L)), 16)
        blk.sync(f)
        dDm = sD.n
        def f(e, dDm=dDm):
            e.wait_ge(sD.h, dDm)
            ins = e.collective_compute("AllGather", ALU.bypass, replica_groups=RG_PAIR,
                                       ins=[dlb_d.ap()], outs=[dlag_d.ap()])
            sCC.inc(ins)
        blk.gpsimd(f)
    cc_x2 = sCC.n
    sl.free("delta", "qn", "kn", "v_own")
    if upto is not None and upto <= 50:
        return nc, sorted(dbg.keys())

    v_halo = sl.alloc("v_halo", 8, T + HLO, BF16)
    scrL = sl.alloc("scrL", 1, T + HLO, BF16)
    facc = sl.alloc("facc", 1, T, F32)
    facc2 = sl.alloc("facc2", 1, T, F32)
    short = sl.alloc("short", 8, T, BF16)
    long_ = sl.alloc("long", 8, T, BF16)
    smean = sl.alloc("smean", 2, T, BF16)
    lmean = sl.alloc("lmean", 2, T, BF16)
    dmean = sl.alloc("dmean", 2, T, BF16)

    with nc.Block() as blk:
        def f(e):
            e.wait_ge(sCC.h, cc_x1)
            _last = e.memset(scrL[:, 0:HLO], 0)
            sV.inc(_last)
        blk.vector(f)
        vz = sV.n
        bl = {}
        for c in range(8):
            def f(e, c=c, vz=vz):
                if c == 0:
                    e.wait_ge(sV.h, vz)
                if c >= 1:
                    e.wait_ge(sV.h, bl[c - 1])
                sD.inc(e.dma_start(out=scrL[:, HLO:HLO + T],
                                   in_=vag_d[c * 128:(c + 1) * 128, 0:T]), 16)
                sD.inc(e.dma_start(out=ch(v_halo, c, T + HLO),
                                   in_=vag_d[c * 128:(c + 1) * 128, T - HLO:L]), 16)
            blk.sync(f)
            dm = sD.n
            def f(e, c=c, dm=dm):
                e.wait_ge(sD.h, dm)
                e.tensor_sub(scrL[:, :], scrL[:, :], ch(v_halo, c, T + HLO))
                e.scalar_tensor_tensor(ch(v_halo, c, T + HLO), scrL[:, :], G_COL,
                                       ch(v_halo, c, T + HLO), ALU.mult, ALU.add)
                _last = e.memset(scrL[:, 0:HLO], 0)
                bl[c] = sV.inc(_last)
            blk.vector(f)
        vhb = sV.n

        def emit_fir(eng, c, sem, accb):
            def f(e, c=c, accb=accb):
                e.wait_ge(sV.h, vhb)
                ws = lambda j: firs[:, c * 3 + j: c * 3 + j + 1]
                wl = lambda j: firl[:, c * 31 + j: c * 31 + j + 1]
                e.tensor_scalar(accb[:, 0:T], ch(v_halo, c, T + HLO, HLO, T),
                                ws(2), None, ALU.mult)
                for j in (1, 0):
                    e.scalar_tensor_tensor(accb[:, 0:T],
                                           ch(v_halo, c, T + HLO, HLO - 2 + j, T),
                                           ws(j), accb[:, 0:T], ALU.mult, ALU.add)
                e.tensor_copy(ch(short, c, T), accb[:, 0:T])
                e.tensor_scalar(accb[:, 0:T], ch(v_halo, c, T + HLO, HLO, T),
                                wl(30), None, ALU.mult)
                for j in range(29, -1, -1):
                    e.scalar_tensor_tensor(accb[:, 0:T],
                                           ch(v_halo, c, T + HLO, HLO - 30 + j, T),
                                           wl(j), accb[:, 0:T], ALU.mult, ALU.add)
                e.tensor_copy(ch(long_, c, T), accb[:, 0:T])
                sem.inc(e.nop())
            getattr(blk, eng)(f)

        for c in range(8):
            emit_fir("vector", c, sV, facc)
        vf, gf = sV.n, sG.n
        def f(e, vf=vf, gf=gf):
            e.wait_ge(sV.h, vf)
            for dc in range(2):
                for t_, strm in ((smean, short), (lmean, long_)):
                    e.tensor_add(ch(t_, dc, T), ch(strm, dc, T), ch(strm, 2 + dc, T))
                    e.tensor_add(ch(t_, dc, T), ch(t_, dc, T), ch(strm, 4 + dc, T))
                    e.tensor_add(ch(t_, dc, T), ch(t_, dc, T), ch(strm, 6 + dc, T))
                    _last = e.tensor_scalar(ch(t_, dc, T), ch(t_, dc, T), 0.25, None, ALU.mult)
            sV.inc(_last)
        blk.vector(f)
        def f(e):
            e.wait_ge(sV.h, sV.n)
            for c in range(8):
                sD.inc(e.dma_start(out=shb_d[c * 128:(c + 1) * 128, :],
                                   in_=ch(short, c, T)), 16)
                sD.inc(e.dma_start(out=lgb_d[c * 128:(c + 1) * 128, :],
                                   in_=ch(long_, c, T)), 16)
            e.wait_ge(sD.h, sD.n)
        blk.sync(f)
    with nc.Block() as blk:
        dump_bf16(blk, "short", short, 8, T)
        dump_bf16(blk, "long", long_, 8, T)
    sl.free("short", "long", "scrL", "facc", "facc2")
    if upto is not None and upto <= 60:
        return nc, sorted(dbg.keys())

    d_own = sl.alloc("d_own", 8, T, BF16)
    dscr = sl.alloc("dscr", 1, T, BF16)
    with nc.Block() as blk:
        bl = {}
        for c in range(8):
            def f(e, c=c):
                if c == 0:
                    e.wait_ge(sCC.h, cc_x2)
                if c >= 1:
                    e.wait_ge(sV.h, bl[c - 1])
                sD.inc(e.dma_start(out=dscr[:, 0:T],
                                   in_=dlag_d[c * 128:(c + 1) * 128, 0:T]), 16)
                sD.inc(e.dma_start(out=ch(d_own, c, T),
                                   in_=dlag_d[c * 128:(c + 1) * 128, T:L]), 16)
            blk.sync(f)
            dm = sD.n
            def f(e, c=c, dm=dm):
                e.wait_ge(sD.h, dm)
                e.tensor_sub(dscr[:, 0:T], dscr[:, 0:T], ch(d_own, c, T))
                _last = e.scalar_tensor_tensor(ch(d_own, c, T), dscr[:, 0:T], G_COL,
                                       ch(d_own, c, T), ALU.mult, ALU.add)
                bl[c] = sV.inc(_last)
            blk.vector(f)
        def f(e):
            for dc in range(2):
                e.tensor_add(ch(dmean, dc, T), ch(d_own, dc, T), ch(d_own, 2 + dc, T))
                e.tensor_add(ch(dmean, dc, T), ch(dmean, dc, T), ch(d_own, 4 + dc, T))
                e.tensor_add(ch(dmean, dc, T), ch(dmean, dc, T), ch(d_own, 6 + dc, T))
                _last = e.tensor_scalar(ch(dmean, dc, T), ch(dmean, dc, T), 0.25, None, ALU.mult)
            sV.inc(_last)
        blk.vector(f)
    with nc.Block() as blk:
        dump_bf16(blk, "d_own", d_own, 8, T)

    hs_own = sl.alloc("hs_own", 8, T, BF16)
    hscr = sl.alloc("hscr", 1, T, BF16)
    with nc.Block() as blk:
        bl2 = {}
        for c in range(8):
            def f(e, c=c):
                if c >= 1:
                    e.wait_ge(sV.h, bl2[c - 1])
                sD.inc(e.dma_start(out=hscr[:, 0:T],
                                   in_=hs_d[c * 128:(c + 1) * 128, 0:T]), 16)
                sD.inc(e.dma_start(out=ch(hs_own, c, T),
                                   in_=hs_d[c * 128:(c + 1) * 128, T:L]), 16)
            blk.sync(f)
            dm = sD.n
            def f(e, c=c, dm=dm):
                e.wait_ge(sD.h, dm)
                e.tensor_sub(hscr[:, 0:T], hscr[:, 0:T], ch(hs_own, c, T))
                _last = e.scalar_tensor_tensor(ch(hs_own, c, T), hscr[:, 0:T], G_COL,
                                               ch(hs_own, c, T), ALU.mult, ALU.add)
                bl2[c] = sV.inc(_last)
            blk.vector(f)

    fg1s = sl.alloc("fg1s", 1, 14 * 1024, BF16)
    fg2s = sl.alloc("fg2s", 1, 32, BF16)
    gel = sl.alloc("gel", 8, NT, BF16)
    lgt = sl.alloc("lgt", 1, T, F32)

    def gate_rhs(kc, nt):
        if kc < 8:
            return ch(hs_own, kc, T, nt * NT, NT)
        if kc < 10:
            return ch(smean, kc - 8, T, nt * NT, NT)
        if kc < 12:
            return ch(lmean, kc - 10, T, nt * NT, NT)
        return ch(dmean, kc - 12, T, nt * NT, NT)

    with nc.Block() as blk:
        def f(e):
            for kc in range(14):
                sG.inc(e.dma_start(out=fg1s[:, kc * 1024:(kc + 1) * 1024],
                                   in_=dI["fg1_t"][kc * 128:(kc + 1) * 128, :]), 16)
            sG.inc(e.dma_start(
                out=fg2s[:, :].rearrange("p (kc m) -> p kc m", m=4),
                in_=dI["fg2_t"][:, :].rearrange("(kc p) m -> p kc m", p=128)), 16)
        blk.gpsimd(f)
        gF = sG.n
        vpre = sV.n
        smark = {}
        lmark = {}
        for nt in range(NTO):
            for mc in range(8):
                idx = nt * 8 + mc
                def f(e, nt=nt, mc=mc, idx=idx, gF=gF, vpre=vpre):
                    if idx == 0:
                        e.wait_ge(sG.h, gF)
                        e.wait_ge(sV.h, vpre)
                    if idx >= 2:
                        e.wait_ge(sS.h, smark[idx - 2])
                    pp = pPP[idx % 2]
                    for kc in range(14):
                        ins = e.matmul(pp[:, 0:NT],
                                       fg1s[:, kc * 1024 + mc * 128: kc * 1024 + mc * 128 + 128],
                                       gate_rhs(kc, nt), start=(kc == 0), stop=(kc == 13))
                    sT.inc(ins)
                blk.tensor(f)
                tg = sT.n
                def f(e, nt=nt, mc=mc, idx=idx, tg=tg):
                    e.wait_ge(sT.h, tg)
                    _last = e.activation(ch(gel, mc, NT), pPP[idx % 2][:, 0:NT],
                                 AF.Gelu, bias=fgb1[:, mc:mc + 1])
                    smark[idx] = sS.inc(_last)
                blk.scalar(f)
            ag = sS.n
            def f(e, nt=nt, ag=ag):
                e.wait_ge(sS.h, ag)
                if nt >= 1:
                    e.wait_ge(sV.h, lmark[nt - 1])
                for kc in range(8):
                    ins = e.matmul(psm[0:4, 0:NT], fg2s[:, kc * 4:(kc + 1) * 4],
                                   ch(gel, kc, NT), start=(kc == 0), stop=(kc == 7))
                sT.inc(ins)
            blk.tensor(f)
            tl = sT.n
            def f(e, nt=nt, tl=tl):
                e.wait_ge(sT.h, tl)
                _last = e.tensor_scalar(lgt[0:4, nt * NT:(nt + 1) * NT], psm[0:4, 0:NT],
                                fgb2[0:4, 0:1], None, ALU.add)
                lmark[nt] = sV.inc(_last)
            blk.vector(f)
    sl.free("fg1s", "fg2s", "gel", "hs_own", "hscr", "smean", "lmean", "dmean", "dscr")
    with nc.Block() as blk:
        dump_row(blk, "logits", lgt[0:4, 0:T], 4, T)
    if upto is not None and upto <= 70:
        return nc, sorted(dbg.keys())

    lgj = [sl.alloc(f"lgj{j}", 1, NCT, F32) for j in range(4)]
    lgjb = [sl.alloc(f"lgjb{j}", 1, NCT, BF16) for j in range(4)]
    wrk = sl.alloc("wrk", 1, 2 * NCT, F32)
    w4row = sl.alloc("w4row", 1, 4 * T, BF16)
    wb4 = [sl.alloc(f"wb{j}", 1, T, BF16) for j in range(4)]

    with nc.Block() as blk:
        def f(e):
            e.activation(scl[0:1, 0:1], ltmp[0:1, 0:1], AF.Exp)
            _last = e.activation(scl[0:1, 0:1], scl[0:1, 0:1], AF.Ln, bias=cst[0:1, 1:2])
            sS.inc(_last)
        blk.scalar(f)
        a0 = sS.n
        def f(e, a0=a0):
            e.wait_ge(sS.h, a0)
            vchain(e,
                lambda e: e.tensor_scalar(scl[0:1, 0:1], scl[0:1, 0:1], 1e-4, None, ALU.add),
                lambda e: e.reciprocal(scl[0:1, 1:2], scl[0:1, 0:1]),
            )
        blk.vector(f)
        vrc = sV.n
        def f(e, vrc=vrc):
            e.wait_ge(sV.h, vrc)
            _last = e.tensor_scalar(scr128[0:1, 0:128], onesf[0:1, :], scl[0:1, 1:2], None, ALU.mult)
            sV.inc(_last)
        blk.vector(f)
        v0 = sV.n
        def f(e, v0=v0):
            e.wait_ge(sV.h, v0)
            ins = e.transpose(pC[:, 0:1], scr128[0:1, 0:128], idf[0:1, 0:1])
            sT.inc(ins)
        blk.tensor(f)
        t0 = sT.n
        def f(e, t0=t0):
            e.wait_ge(sT.h, t0)
            _last = e.tensor_copy(sclc[:, 0:1], pC[:, 0:1])
            sV.inc(_last)
        blk.vector(f)
        vmark = {}
        for i in range(NCT):
            def f(e, i=i):
                if i >= 2:
                    e.wait_ge(sV.h, vmark[i - 2])
                ins = e.transpose(pPP[i % 2][:, 0:4],
                                  lgt[0:4, i * 128:(i + 1) * 128], idf[0:4, 0:4])
                sT.inc(ins)
            blk.tensor(f)
            ti = sT.n
            def f(e, i=i, ti=ti):
                e.wait_ge(sT.h, ti)
                for j in range(4):
                    _last = e.tensor_copy(lgj[j][:, i:i + 1], pPP[i % 2][:, j:j + 1])
                vmark[i] = sV.inc(_last)
            blk.vector(f)
        vt = sV.n
        def f(e, vt=vt):
            e.wait_ge(sV.h, vt)
            vchain(e,
                lambda e: e.tensor_tensor(wrk[:, 0:NCT], lgj[0][:, :], lgj[1][:, :], ALU.max),
                lambda e: e.tensor_tensor(wrk[:, NCT:2 * NCT], lgj[2][:, :], lgj[3][:, :], ALU.max),
                lambda e: e.tensor_tensor(wrk[:, 0:NCT], wrk[:, 0:NCT], wrk[:, NCT:2 * NCT], ALU.max),
                *[ (lambda e, j=j: e.tensor_sub(lgj[j][:, :], lgj[j][:, :], wrk[:, 0:NCT])) for j in range(4) ],
                *[ (lambda e, j=j: e.tensor_scalar(lgj[j][:, :], lgj[j][:, :], sclc[:, 0:1], None, ALU.mult)) for j in range(4) ],
            )
        blk.vector(f)
        v1 = sV.n
        def f(e, v1=v1):
            e.wait_ge(sV.h, v1)
            for j in range(4):
                _last = e.activation(lgj[j][:, :], lgj[j][:, :], AF.Exp)
            sS.inc(_last)
        blk.scalar(f)
        a1 = sS.n
        def f(e, a1=a1):
            e.wait_ge(sS.h, a1)
            vchain(e,
                lambda e: e.tensor_tensor(wrk[:, 0:NCT], lgj[0][:, :], lgj[1][:, :], ALU.add),
                lambda e: e.tensor_tensor(wrk[:, 0:NCT], wrk[:, 0:NCT], lgj[2][:, :], ALU.add),
                lambda e: e.tensor_tensor(wrk[:, 0:NCT], wrk[:, 0:NCT], lgj[3][:, :], ALU.add),
                lambda e: e.reciprocal(wrk[:, 0:NCT], wrk[:, 0:NCT]),
                *[ (lambda e, j=j: e.tensor_tensor(lgj[j][:, :], lgj[j][:, :], wrk[:, 0:NCT], ALU.mult)) for j in range(4) ],
                lambda e: e.tensor_scalar(wrk[:, NCT:2 * NCT], lgj[3][:, :], 0.05, None, ALU.mult),
                lambda e: e.tensor_tensor(lgj[0][:, :], lgj[0][:, :], wrk[:, NCT:2 * NCT], ALU.max),
                lambda e: e.tensor_tensor(lgj[1][:, :], lgj[1][:, :], wrk[:, NCT:2 * NCT], ALU.max),
                lambda e: e.tensor_tensor(wrk[:, 0:NCT], lgj[0][:, :], lgj[1][:, :], ALU.add),
                lambda e: e.tensor_tensor(wrk[:, 0:NCT], wrk[:, 0:NCT], lgj[2][:, :], ALU.add),
                lambda e: e.tensor_tensor(wrk[:, 0:NCT], wrk[:, 0:NCT], lgj[3][:, :], ALU.add),
                lambda e: e.reciprocal(wrk[:, 0:NCT], wrk[:, 0:NCT]),
                *[ (lambda e, j=j: e.tensor_tensor(lgj[j][:, :], lgj[j][:, :], wrk[:, 0:NCT], ALU.mult)) for j in range(4) ],
                *[ (lambda e, j=j: e.tensor_copy(lgjb[j][:, :], lgj[j][:, :])) for j in range(4) ],
            )
        blk.vector(f)
        v2 = sV.n
        vmark = {}
        for i in range(NCT):
            for j in range(4):
                k = i * 4 + j
                def f(e, i=i, j=j, k=k, v2=v2):
                    if k == 0:
                        e.wait_ge(sV.h, v2)
                    if k >= 2:
                        e.wait_ge(sV.h, vmark[k - 2])
                    ins = e.matmul(pPP[k % 2][0:1, 0:128],
                                   lgjb[j][:, i:i + 1], idb[:, :])
                    sT.inc(ins)
                blk.tensor(f)
                ti = sT.n
                def f(e, i=i, j=j, k=k, ti=ti):
                    e.wait_ge(sT.h, ti)
                    _last = e.tensor_copy(w4row[0:1, j * T + i * 128: j * T + (i + 1) * 128],
                                  pPP[k % 2][0:1, 0:128])
                    vmark[k] = sV.inc(_last)
                blk.vector(f)
        vw = sV.n
        smark = {}
        for j in range(4):
            for nt in range(NTO):
                idx = j * NTO + nt
                def f(e, j=j, nt=nt, idx=idx, vw=vw):
                    if idx == 0:
                        e.wait_ge(sV.h, vw)
                    if idx >= 2:
                        e.wait_ge(sS.h, smark[idx - 2])
                    ins = e.matmul(pPP[idx % 2][:, 0:NT], on1[0:1, :],
                                   w4row[0:1, j * T + nt * NT: j * T + (nt + 1) * NT])
                    sT.inc(ins)
                blk.tensor(f)
                tj = sT.n
                def f(e, j=j, nt=nt, idx=idx, tj=tj):
                    e.wait_ge(sT.h, tj)
                    _last = e.copy(wb4[j][:, nt * NT:(nt + 1) * NT], pPP[idx % 2][:, 0:NT])
                    smark[idx] = sS.inc(_last)
                blk.scalar(f)
    with nc.Block() as blk:
        dump_row(blk, "wgt", w4row[0:1, 0:4 * T], 1, 4 * T)
    sl.free("lgj0", "lgj1", "lgj2", "lgj3", "lgjb0", "lgjb1", "lgjb2", "lgjb3",
            "wrk", "lgt", "w4row")
    if upto is not None and upto <= 80:
        return nc, sorted(dbg.keys())

    o_t = sl.alloc("o_t", 8, T, BF16)
    sht = sl.alloc("sht", 2, T, BF16)
    lgtl = sl.alloc("lgtl", 2, T, BF16)
    svt = sl.alloc("svt", 2, T, BF16)
    ssq4 = sl.alloc("ssq4", 1, T, F32)
    rrow4 = sl.alloc("rrow4", 1, T, BF16)
    rbh = sl.alloc("rbh", 1, T, BF16)

    with nc.Block() as blk:
        relo = {}
        for c in range(8):
            bi = c % 2
            def f(e, c=c, bi=bi):
                if c >= 2:
                    e.wait_ge(sV.h, relo[("v", c - 2)])
                    e.wait_ge(sP.h, relo[("g", c - 2)])
                sF[c % 2].inc(e.dma_start(out=ch(sht, bi, T),
                                   in_=shb_d[c * 128:(c + 1) * 128, :]), 16)
                ins = e.dma_start(out=ch(lgtl, bi, T),
                                  in_=lgb_d[c * 128:(c + 1) * 128, :])
                relo[("d", c)] = sF[c % 2].inc(ins, 16)
            blk.sync(f)
            def f(e, c=c, bi=bi):
                e.wait_ge(sF[c % 2].h, relo[("d", c)])
                e.tensor_tensor(ch(o_t, c, T), ch(sht, bi, T), wb4[0][:, 0:T], ALU.mult)
                e.tensor_tensor(ch(lgtl, bi, T), ch(lgtl, bi, T), wb4[1][:, 0:T], ALU.mult)
                _last = e.tensor_add(ch(o_t, c, T), ch(o_t, c, T), ch(lgtl, bi, T))
                relo[("v", c)] = sV.inc(_last)
            blk.vector(f)
            def f(e, c=c, bi=bi):
                e.wait_ge(sV.h, relo[("v", c)])
                e.tensor_tensor(ch(svt, bi, T), ch(d_own, c, T), wb4[2][:, 0:T], ALU.mult)
                e.tensor_add(ch(o_t, c, T), ch(o_t, c, T), ch(svt, bi, T))
                e.tensor_tensor(ch(svt, bi, T), ch(v_halo, c, T + HLO, HLO, T),
                                wb4[3][:, 0:T], ALU.mult)
                _last = e.tensor_add(ch(o_t, c, T), ch(o_t, c, T), ch(svt, bi, T))
                relo[("g", c)] = sP.inc(_last)
            blk.gpsimd(f)
        def f(e):
            e.wait_ge(sP.h, sP.n)
        blk.vector(f)
    with nc.Block() as blk:
        dump_bf16(blk, "o_pre", o_t, 8, T)

    with nc.Block() as blk:
        for hh in range(4):
            tmark = {}
            for nt in range(NTO):
                for ci in range(2):
                    c = hh * 2 + ci
                    k = nt * 2 + ci
                    def f(e, c=c, ci=ci, k=k):
                        if k >= 2:
                            e.wait_ge(sT.h, tmark[k - 2])
                        _last = e.activation(sqb[ci][:, :], ch(o_t, c, T, (k // 2) * NT, NT),
                                     AF.Square)
                        sS.inc(_last)
                    blk.scalar(f)
                aq = sS.n
                def f(e, nt=nt, aq=aq):
                    e.wait_ge(sS.h, aq)
                    if nt >= 1:
                        e.wait_ge(sV.h, sV.n)
                    for ci in range(2):
                        ins = e.matmul(psm[0:1, 0:NT], onk[:, 1:2], sqb[ci][:, :],
                                       start=(ci == 0), stop=(ci == 1))
                    tmark[nt * 2] = tmark[nt * 2 + 1] = sT.inc(ins)
                blk.tensor(f)
                tq = sT.n
                def f(e, nt=nt, tq=tq):
                    e.wait_ge(sT.h, tq)
                    _last = e.tensor_copy(ssq4[0:1, nt * NT:(nt + 1) * NT], psm[0:1, 0:NT])
                    sV.inc(_last)
                blk.vector(f)
            vq = sV.n
            def f(e, vq=vq):
                e.wait_ge(sV.h, vq)
                _last = e.activation(ssq4[0:1, 0:T], ssq4[0:1, 0:T], AF.Sqrt, bias=EPS_RMS)
                sS.inc(_last)
            blk.scalar(f)
            ar = sS.n
            def f(e, ar=ar):
                e.wait_ge(sS.h, ar)
                e.reciprocal(ssq4[0:1, 0:T], ssq4[0:1, 0:T])
                _last = e.tensor_copy(rrow4[0:1, 0:T], ssq4[0:1, 0:T])
                sV.inc(_last)
            blk.vector(f)
            vr = sV.n
            smark = {}
            for nt in range(NTO):
                def f(e, nt=nt, vr=vr):
                    if nt == 0:
                        e.wait_ge(sV.h, vr)
                    if nt >= 2:
                        e.wait_ge(sS.h, smark[nt - 2])
                    ins = e.matmul(pPP[nt % 2][:, 0:NT], on1[0:1, :],
                                   rrow4[0:1, nt * NT:(nt + 1) * NT])
                    sT.inc(ins)
                blk.tensor(f)
                tb = sT.n
                def f(e, nt=nt, tb=tb):
                    e.wait_ge(sT.h, tb)
                    _last = e.copy(rbh[:, nt * NT:(nt + 1) * NT], pPP[nt % 2][:, 0:NT])
                    smark[nt] = sS.inc(_last)
                blk.scalar(f)
            ab = sS.n
            def f(e, hh=hh, ab=ab):
                e.wait_ge(sS.h, ab)
                for ci in range(2):
                    c = hh * 2 + ci
                    _last = e.scalar_tensor_tensor(ch(o_t, c, T), ch(o_t, c, T),
                                           gamc[:, c:c + 1], rbh[:, 0:T],
                                           ALU.mult, ALU.mult)
                sV.inc(_last)
            blk.vector(f)
    with nc.Block() as blk:
        dump_bf16(blk, "o_rms", o_t, 8, T)
    sl.free("sht", "lgtl", "svt", "ssq4", "rrow4", "rbh", "d_own", "v_halo",
            "wb0", "wb1", "wb2", "wb3")
    if upto is not None and upto <= 90:
        return nc, sorted(dbg.keys())

    opjs = sl.alloc("opjs", 1, 8 * 1024, BF16)
    x2 = sl.alloc("x2", 8, T, BF16)
    xot = sl.alloc("xot", 2, NT, F32)
    with nc.Block() as blk:
        def f(e):
            for kc in range(8):
                sG.inc(e.dma_start(out=opjs[:, kc * 1024:(kc + 1) * 1024],
                                   in_=dI["opj_t"][kc * 128:(kc + 1) * 128, :]), 16)
        blk.gpsimd(f)
        gO = sG.n
        xw = {}
        vpre = sV.n
        for mc in range(8):
            for nt in range(NTO):
                idx = mc * NTO + nt
                def f(e, mc=mc, nt=nt, idx=idx):
                    if idx >= 2:
                        e.wait_ge(sV.h, xw[("v", idx - 2)])
                    ins = e.dma_start(out=ch(xot, idx % 2, NT),
                                      in_=dI["x_own"][mc * 128:(mc + 1) * 128,
                                                      nt * NT:(nt + 1) * NT])
                    xw[("d", idx)] = sF[idx % 2].inc(ins, 16)
                blk.sync(f)
                def f(e, mc=mc, nt=nt, idx=idx, gO=gO, vpre=vpre):
                    if idx == 0:
                        e.wait_ge(sG.h, gO)
                        e.wait_ge(sV.h, vpre)
                    if idx >= 2:
                        e.wait_ge(sV.h, xw[("v", idx - 2)])
                    for kc in range(8):
                        ins = e.matmul(pPP[idx % 2][:, 0:NT],
                                       opjs[:, kc * 1024 + mc * 128: kc * 1024 + mc * 128 + 128],
                                       ch(o_t, kc, T, nt * NT, NT),
                                       start=(kc == 0), stop=(kc == 7))
                    sT.inc(ins)
                blk.tensor(f)
                tm = sT.n
                def f(e, mc=mc, nt=nt, idx=idx, tm=tm):
                    e.wait_ge(sT.h, tm)
                    e.wait_ge(sF[idx % 2].h, xw[("d", idx)])
                    _last = e.tensor_add(ch(x2, mc, T, nt * NT, NT), pPP[idx % 2][:, 0:NT],
                                 ch(xot, idx % 2, NT))
                    xw[("v", idx)] = sV.inc(_last)
                blk.vector(f)
        def f(e):
            e.wait_ge(sV.h, sV.n)
            for c in range(8):
                sD.inc(e.dma_start(out=x2b_d[c * 128:(c + 1) * 128, :],
                                   in_=ch(x2, c, T)), 16)
            e.wait_ge(sD.h, sD.n)
        blk.sync(f)
    with nc.Block() as blk:
        dump_bf16(blk, "x2", x2, 8, T)
    sl.free("opjs", "o_t", "xot")
    if upto is not None and upto <= 100:
        return nc, sorted(dbg.keys())

    ln2 = sl.alloc("ln2", 8, T, BF16)
    ln_pass(x2, 8, T, NTO, ln2)
    sl.free("x2")
    with nc.Block() as blk:
        dump_bf16(blk, "ln2", ln2, 8, T)

    xq = sl.alloc("xq", 8, T, BF16)
    qtf = sl.alloc("qtf", 1, T, F32)
    qti = sl.alloc("qti", 1, T, I32)

    def scalar_to_col(blk, src_ap, col_idx):
        vpre_stc = sV.n
        def f(e, src_ap=src_ap, vpre_stc=vpre_stc):
            e.wait_ge(sV.h, vpre_stc)
            e.wait_ge(sT.h, sT.n)
            _last = e.tensor_scalar(scr128[0:1, 0:128], onesf[0:1, :], src_ap, None, ALU.mult)
            sV.inc(_last)
        blk.vector(f)
        v_ = sV.n
        def f(e, v_=v_):
            e.wait_ge(sV.h, v_)
            ins = e.transpose(pC[:, 0:1], scr128[0:1, 0:128], idf[0:1, 0:1])
            sT.inc(ins)
        blk.tensor(f)
        t_ = sT.n
        def f(e, t_=t_, col_idx=col_idx):
            e.wait_ge(sT.h, t_)
            _last = e.tensor_copy(sclc[:, col_idx:col_idx + 1], pC[:, 0:1])
            sV.inc(_last)
        blk.vector(f)

    with nc.Block() as blk:
        def f(e):
            for c in range(8):
                _last = e.reduce_max(mxc[:, c:c + 1], ch(ln2, c, T), axis=AX,
                                     apply_absolute_value=True)
            sV.inc(_last)
            e.wait_ge(sV.h, sV.n)
            _last = e.reduce_max(wmc[:, 4:5], mxc[:, 0:8], axis=AX)
            sV.inc(_last)
        blk.vector(f)
        vm = sV.n
        def f(e, vm=vm):
            e.wait_ge(sV.h, vm)
            vchain(e,
                *[ (lambda e, k=k: e.tensor_copy(mxc[:, 40 + k:41 + k], wmc[:, 4:5])) for k in range(4) ],
                *[ (lambda e, k=k: e.tensor_copy(mxc[:, 44 + k:45 + k], wmc[:, 0:1])) for k in range(4) ],
                *[ (lambda e, k=k: e.tensor_copy(mxc[:, 48 + k:49 + k], wmc[:, 1:2])) for k in range(4) ],
            )
        blk.vector(f)
        vm2 = sV.n
        tprev = [0]
        for k, (srcc, dstc) in enumerate([(40, 2), (44, 8), (48, 9)]):
            def f(e, k=k, srcc=srcc, vm2=vm2):
                if k == 0:
                    e.wait_ge(sV.h, vm2)
                else:
                    e.wait_ge(sV.h, sV.n)
                ins = e.transpose(psm[0:4, 0:128], mxc[:, srcc:srcc + 4], idf[:, :])
                sT.inc(ins)
            blk.tensor(f)
            tpp = sT.n
            def f(e, dstc=dstc, tpp=tpp):
                e.wait_ge(sT.h, tpp)
                _last = e.reduce_max(scl[0:1, dstc:dstc + 1], psm[0:1, 0:128], axis=AX)
                sV.inc(_last)
            blk.vector(f)
        vmx = sV.n
        def f(e, vmx=vmx):
            e.wait_ge(sV.h, vmx)
            sD.inc(e.dma_start(out=mx_in[:, :], in_=scl[0:1, 2:3]), 16)
        blk.sync(f)
        dmx = sD.n
        def f(e, dmx=dmx):
            e.wait_ge(sD.h, dmx)
            ins = e.collective_compute("AllReduce", ALU.max, replica_groups=RG_ALL,
                                       ins=[mx_in.ap()], outs=[mx_out.ap()])
            sCC.inc(ins)
        blk.gpsimd(f)
        ccm = sCC.n
        def f(e, ccm=ccm):
            e.wait_ge(sCC.h, ccm)
            sD.inc(e.dma_start(out=scl[0:1, 2:3], in_=mx_out[:, :]), 16)
        blk.sync(f)
        dmx2 = sD.n
        def f(e, dmx2=dmx2):
            e.wait_ge(sD.h, dmx2)
            vchain(e,
                lambda e: e.tensor_scalar(scl[0:1, 2:3], scl[0:1, 2:3], 1e-6, None, ALU.max),
                lambda e: e.tensor_scalar(scl[0:1, 3:4], scl[0:1, 2:3], C127, None, ALU.mult),
                lambda e: e.reciprocal(scl[0:1, 4:5], scl[0:1, 3:4]),
                lambda e: e.tensor_scalar(scl[0:1, 8:9], scl[0:1, 8:9], 1e-6, None, ALU.max),
                lambda e: e.tensor_scalar(scl[0:1, 5:6], scl[0:1, 8:9], C127, None, ALU.mult),
                lambda e: e.reciprocal(scl[0:1, 6:7], scl[0:1, 5:6]),
                lambda e: e.tensor_tensor(scl[0:1, 7:8], scl[0:1, 5:6], scl[0:1, 3:4], ALU.mult),
                lambda e: e.tensor_scalar(scl[0:1, 9:10], scl[0:1, 9:10], 1e-6, None, ALU.max),
                lambda e: e.tensor_scalar(scl[0:1, 10:11], scl[0:1, 9:10], C127, None, ALU.mult),
                lambda e: e.reciprocal(scl[0:1, 11:12], scl[0:1, 10:11]),
            )
        blk.vector(f)
        scalar_to_col(blk, scl[0:1, 4:5], 0)
        scalar_to_col(blk, scl[0:1, 7:8], 1)
        scalar_to_col(blk, scl[0:1, 6:7], 4)
        scalar_to_col(blk, scl[0:1, 11:12], 5)
        vcol = sV.n
        def f(e, vcol=vcol):
            e.wait_ge(sV.h, vcol)
            e.tensor_scalar(b1s[:, :], b1cs[:, :], sclc[:, 1:2], None, ALU.mult)
            for c in range(8):
                e.tensor_scalar(qtf[:, 0:T], ch(ln2, c, T), sclc[:, 0:1], None, ALU.mult)
                e.tensor_copy(qti[:, 0:T], qtf[:, 0:T])
                _last = e.tensor_copy(ch(xq, c, T), qti[:, 0:T])
            sV.inc(_last)
        blk.vector(f)
    sl.free("ln2")
    with nc.Block() as blk:
        dump_bf16(blk, "xq", xq, 8, T)
    if upto is not None and upto <= 110:
        return nc, sorted(dbg.keys())

    hbuf = sl.alloc("hbuf", 32, T, BF16)
    w1q = [sl.alloc("w1qa", 1, 8 * 128, BF16), sl.alloc("w1qb", 1, 8 * 128, BF16)]
    w1f = sl.alloc("w1f", 1, 1024, F32)
    w1i = sl.alias("w1i", "w1f", 1, 1024, I32)
    mm1 = {}
    with nc.Block() as blk:
        for mc in range(32):
            def f(e, mc=mc):
                if mc >= 1:
                    e.wait_ge(sP.h, mm1[("g", mc - 1)])
                ins = e.dma_start(out=w1f[:, :].rearrange("p (kc n) -> p kc n", n=128),
                                  in_=dI["w1_t"][:, mc * 128:(mc + 1) * 128].rearrange(
                                      "(kc p) n -> p kc n", p=128))
                mm1[("d", mc)] = sW.inc(ins, 16)
            blk.sync(f)
            def f(e, mc=mc):
                e.wait_ge(sW.h, mm1[("d", mc)])
                if mc >= 2:
                    e.wait_ge(sT.h, mm1[("t", mc - 2)])
                e.tensor_scalar(w1f[:, :], w1f[:, :], sclc[:, 4:5], None, ALU.mult)
                e.tensor_copy(w1i[:, :], w1f[:, :])
                _last = e.tensor_copy(w1q[mc % 2][:, :], w1i[:, :])
                mm1[("g", mc)] = sP.inc(_last)
            blk.vector(f)
            for nt in range(NTO):
                idx = mc * NTO + nt
                def f(e, mc=mc, nt=nt, idx=idx):
                    e.wait_ge(sP.h, mm1[("g", mc)])
                    if idx >= 2:
                        e.wait_ge(sS.h, mm1[("s", idx - 2)])
                    for kc in range(8):
                        ins = e.matmul(pPP[idx % 2][:, 0:NT],
                                       w1q[mc % 2][:, kc * 128:(kc + 1) * 128],
                                       ch(xq, kc, T, nt * NT, NT),
                                       start=(kc == 0), stop=(kc == 7))
                    mm1[("t", mc)] = sT.inc(ins)
                blk.tensor(f)
                tm = sT.n
                def f(e, mc=mc, nt=nt, idx=idx, tm=tm):
                    e.wait_ge(sT.h, tm)
                    _last = e.activation(ch(hbuf, mc, T, nt * NT, NT), pPP[idx % 2][:, 0:NT],
                                 AF.Gelu, bias=b1s[:, mc:mc + 1], scale=sclc[:, 1:2])
                    mm1[("s", idx)] = sS.inc(_last)
                blk.scalar(f)
    sl.free("xq", "w1qa", "w1qb", "w1f")
    with nc.Block() as blk:
        dump_bf16(blk, "hffn", hbuf, 32, T)
    if upto is not None and upto <= 130:
        return nc, sorted(dbg.keys())

    with nc.Block() as blk:
        def f(e):
            for c in range(32):
                _last = e.reduce_max(mxc[:, c:c + 1], ch(hbuf, c, T), axis=AX,
                                     apply_absolute_value=True)
            sV.inc(_last)
            e.wait_ge(sV.h, sV.n)
            _last = e.reduce_max(wmc[:, 4:5], mxc[:, 0:32], axis=AX)
            sV.inc(_last)
        blk.vector(f)
        vm = sV.n
        def f(e, vm=vm):
            e.wait_ge(sV.h, vm)
            vchain(e,
                *[ (lambda e, k=k: e.tensor_copy(mxc[:, 40 + k:41 + k], wmc[:, 4:5])) for k in range(4) ],
            )
        blk.vector(f)
        vm2 = sV.n
        def f(e, vm2=vm2):
            e.wait_ge(sV.h, vm2)
            ins = e.transpose(psm[0:4, 0:128], mxc[:, 40:44], idf[:, :])
            sT.inc(ins)
        blk.tensor(f)
        tm = sT.n
        def f(e, tm=tm):
            e.wait_ge(sT.h, tm)
            _last = e.reduce_max(scl[0:1, 12:13], psm[0:1, 0:128], axis=AX)
            sV.inc(_last)
        blk.vector(f)
        vx = sV.n
        def f(e, vx=vx):
            e.wait_ge(sV.h, vx)
            sD.inc(e.dma_start(out=mh_in[:, :], in_=scl[0:1, 12:13]), 16)
        blk.sync(f)
        dh = sD.n
        def f(e, dh=dh):
            e.wait_ge(sD.h, dh)
            ins = e.collective_compute("AllReduce", ALU.max, replica_groups=RG_ALL,
                                       ins=[mh_in.ap()], outs=[mh_out.ap()])
            sCC.inc(ins)
        blk.gpsimd(f)
        cch = sCC.n
        def f(e, cch=cch):
            e.wait_ge(sCC.h, cch)
            sD.inc(e.dma_start(out=scl[0:1, 12:13], in_=mh_out[:, :]), 16)
        blk.sync(f)
        dh2 = sD.n
        def f(e, dh2=dh2):
            e.wait_ge(sD.h, dh2)
            vchain(e,
                lambda e: e.tensor_scalar(scl[0:1, 12:13], scl[0:1, 12:13], 1e-6, None, ALU.max),
                lambda e: e.tensor_scalar(scl[0:1, 13:14], scl[0:1, 12:13], C127, None, ALU.mult),
                lambda e: e.reciprocal(scl[0:1, 14:15], scl[0:1, 13:14]),
                lambda e: e.tensor_tensor(scl[0:1, 15:16], scl[0:1, 10:11], scl[0:1, 13:14], ALU.mult),
            )
        blk.vector(f)
        scalar_to_col(blk, scl[0:1, 14:15], 2)
        scalar_to_col(blk, scl[0:1, 15:16], 3)
        vcl = sV.n
        def f(e, vcl=vcl):
            e.wait_ge(sV.h, vcl)
            _last = e.tensor_scalar(b2s[:, :], b2cs[:, :], sclc[:, 3:4], None, ALU.mult)
            sV.inc(_last)
        blk.vector(f)
        vq0 = sV.n
        for c in range(32):
            def f(e, c=c, vq0=vq0):
                if c == 0:
                    e.wait_ge(sV.h, vq0)
                e.tensor_scalar(qtf[:, 0:T], ch(hbuf, c, T), sclc[:, 2:3], None, ALU.mult)
                e.tensor_copy(qti[:, 0:T], qtf[:, 0:T])
                _last = e.tensor_copy(ch(hbuf, c, T), qti[:, 0:T])
                sV.inc(_last)
            blk.vector(f)
        hq_v, hq_g = sV.n, sP.n
    with nc.Block() as blk:
        dump_bf16(blk, "hq", hbuf, 32, T)
        dump_row(blk, "fsc", sclc[0:1, 0:8], 1, 8)
        dump_row(blk, "sclr", scl[0:1, 0:16], 1, 16)
        dump_row(blk, "b2sr", b2s[0:128, 0:8], 128, 8)

    sl.free("qtf", "qti")
    w2q = [sl.alloc("w2qa", 1, 32 * 128, BF16), sl.alloc("w2qb", 1, 32 * 128, BF16)]
    w2f = sl.alloc("w2f", 1, 4096, F32)
    w2i = sl.alias("w2i", "w2f", 1, 4096, I32)
    x2t = sl.alloc("x2t", 2, NT, BF16)
    outb = sl.alloc("outb", 2, NT, F32)
    mm2 = {}
    with nc.Block() as blk:
        def f(e, hq_v=hq_v, hq_g=hq_g):
            e.wait_ge(sV.h, hq_v)
            e.wait_ge(sP.h, hq_g)
            _last = e.nop()
            sV.inc(_last)
        blk.vector(f)
        hq_all = sV.n
        for mc in range(8):
            def f(e, mc=mc):
                if mc >= 1:
                    e.wait_ge(sP.h, mm2[("g", mc - 1)])
                ins = e.dma_start(out=w2f[:, :].rearrange("p (kc n) -> p kc n", n=128),
                                  in_=dI["w2_t"][:, mc * 128:(mc + 1) * 128].rearrange(
                                      "(kc p) n -> p kc n", p=128))
                mm2[("d", mc)] = sW.inc(ins, 16)
            blk.sync(f)
            def f(e, mc=mc):
                e.wait_ge(sW.h, mm2[("d", mc)])
                if mc >= 2:
                    e.wait_ge(sT.h, mm2[("t", mc - 2)])
                e.tensor_scalar(w2f[:, :], w2f[:, :], sclc[:, 5:6], None, ALU.mult)
                e.tensor_copy(w2i[:, :], w2f[:, :])
                _last = e.tensor_copy(w2q[mc % 2][:, :], w2i[:, :])
                mm2[("g", mc)] = sP.inc(_last)
            blk.vector(f)
            for nt in range(NTO):
                idx = mc * NTO + nt
                def f(e, mc=mc, nt=nt, idx=idx):
                    if idx >= 2:
                        e.wait_ge(sV.h, mm2[("vo", idx - 2)])
                    ins = e.dma_start(out=ch(x2t, idx % 2, NT),
                                      in_=x2b_d[mc * 128:(mc + 1) * 128,
                                                nt * NT:(nt + 1) * NT])
                    mm2[("dx", idx)] = sF[idx % 2].inc(ins, 16)
                blk.sync(f)
                def f(e, mc=mc, nt=nt, idx=idx, hq_all=hq_all):
                    e.wait_ge(sP.h, mm2[("g", mc)])
                    if idx == 0:
                        e.wait_ge(sV.h, hq_all)
                    if idx >= 2:
                        e.wait_ge(sV.h, mm2[("vo", idx - 2)])
                    for kc in range(32):
                        ins = e.matmul(pPP[idx % 2][:, 0:NT],
                                       w2q[mc % 2][:, kc * 128:(kc + 1) * 128],
                                       ch(hbuf, kc, T, nt * NT, NT),
                                       start=(kc == 0), stop=(kc == 31))
                    mm2[("t", mc)] = sT.inc(ins)
                blk.tensor(f)
                tm = sT.n
                def f(e, mc=mc, nt=nt, idx=idx, tm=tm):
                    e.wait_ge(sT.h, tm)
                    e.wait_ge(sF[idx % 2].h, mm2[("dx", idx)])
                    if idx >= 2:
                        e.wait_ge(sOp[idx % 2].h, mm2[("o", idx - 2)])
                    e.tensor_scalar(ch(outb, idx % 2, NT), pPP[idx % 2][:, 0:NT],
                                    sclc[:, 3:4], b2s[:, mc:mc + 1], ALU.mult, ALU.add)
                    _last = e.tensor_add(ch(outb, idx % 2, NT), ch(outb, idx % 2, NT),
                                 ch(x2t, idx % 2, NT))
                    mm2[("vo", idx)] = sV.inc(_last)
                blk.vector(f)
                def f(e, mc=mc, nt=nt, idx=idx):
                    e.wait_ge(sV.h, mm2[("vo", idx)])
                    mm2[("o", idx)] = sOp[idx % 2].inc(e.dma_start(
                        out=out_d[mc * 128:(mc + 1) * 128, nt * NT:(nt + 1) * NT],
                        in_=ch(outb, idx % 2, NT)), 16)
                blk.scalar(f)
        def f(e):
            e.wait_ge(sOp[0].h, sOp[0].n)
            e.wait_ge(sOp[1].h, sOp[1].n)
        blk.scalar(f)

    return nc, sorted(dbg.keys())


def stage_inputs(inputs, L=4096):
    T = L // 2
    f32 = np.float32
    g = {k: np.asarray(v) for k, v in inputs.items()}
    x = np.asarray(g["L0_attn_hidden_states"], f32)[:, :L, :]

    def wT(w):
        return np.ascontiguousarray(np.asarray(w, f32).T)

    consts = np.zeros((128, 8), f32)
    consts[:, 0] = 1e-5; consts[:, 1] = 1.0; consts[:, 2] = 1e-12
    consts[:, 3] = 1e-5; consts[:, 4] = 1e-4
    onesk = np.zeros((128, 4), f32)
    onesk[:, 0] = 1.0 / 1024; onesk[:, 1] = 1.0 / 256; onesk[:, 2] = 1.0
    identb = np.eye(128, dtype=ml_dtypes.bfloat16)
    identf = np.eye(128, dtype=f32)
    mask_lo = np.tril(np.ones((128, 128)), -1).astype(ml_dtypes.bfloat16)
    mask_up = np.triu(np.ones((128, 128)), 0).astype(ml_dtypes.bfloat16)
    ones1 = np.ones((1, 128), ml_dtypes.bfloat16)
    onesf = np.ones((1, 128), f32)
    gam = np.tile(np.asarray(g["L0_attn_o_norm_w"], f32), 4)
    gamc = np.ascontiguousarray(gam.reshape(8, 128).T)
    fg1b = np.ascontiguousarray(np.asarray(g["L0_attn_fg_b1"], f32).reshape(8, 128).T)
    b1c = np.ascontiguousarray(np.asarray(g["L0_ffn_b1"], f32).reshape(32, 128).T)
    b2c = np.ascontiguousarray(np.asarray(g["L0_ffn_b2"], f32).reshape(8, 128).T)

    fir_s = np.asarray(g["L0_attn_fir_short_w"], f32).reshape(1024, 3)
    fir_l = np.asarray(g["L0_attn_fir_long_w"], f32).reshape(1024, 31)
    fir_s_c = np.zeros((128, 24), f32)
    fir_l_c = np.zeros((128, 248), f32)
    for c in range(8):
        fir_s_c[:, c * 3:(c + 1) * 3] = fir_s[c * 128:(c + 1) * 128, :]
        fir_l_c[:, c * 31:(c + 1) * 31] = fir_l[c * 128:(c + 1) * 128, :]

    maps = []
    for core in range(8):
        b, half = core // 2, core % 2
        own_heads = [0, 1] if half == 0 else [2, 3]
        rows = np.concatenate([np.arange(h * 256, (h + 1) * 256) for h in own_heads])

        cw = np.zeros((128, 48), f32)
        for p_idx, wname in enumerate(["L0_attn_q_conv_w", "L0_attn_k_conv_w",
                                       "L0_attn_v_conv_w"]):
            w = np.asarray(g[wname], f32)[rows, :]
            for mc in range(4):
                cw[:, p_idx * 16 + mc * 4:(p_idx * 16 + mc * 4) + 4] = \
                    w[mc * 128:(mc + 1) * 128, :]

        flags = np.zeros((128, 4), f32)
        flags[:, 0] = float(half)
        flags[:, 1] = 1.0 - float(half)
        flags[:, 2] = 1.0 / 127.0

        m = {
            "x_fm": np.ascontiguousarray(x[b].T),
            "x_own": np.ascontiguousarray(x[b, half * T:(half + 1) * T].T),
            "wq_t": wT(np.asarray(g["L0_attn_q_proj_w"], f32)[rows]),
            "wk_t": wT(np.asarray(g["L0_attn_k_proj_w"], f32)[rows]),
            "wv_t": wT(np.asarray(g["L0_attn_v_proj_w"], f32)[rows]),
            "wb_t": wT(np.asarray(g["L0_attn_b_proj_w"], f32)[own_heads]),
            "cw": cw, "fir_s": fir_s_c, "fir_l": fir_l_c,
            "fg1_t": wT(g["L0_attn_fg_w1"]),
            "fg1_b": fg1b,
            "fg2_t": wT(g["L0_attn_fg_w2"]),
            "fg2_b": np.asarray(g["L0_attn_fg_b2"], f32).reshape(4, 1),
            "ltemp": np.asarray(g["L0_attn_logit_temp"], f32).reshape(1, 1),
            "gam": gamc,
            "opj_t": wT(g["L0_attn_o_proj_w"]),
            "w1_t": wT(g["L0_ffn_w1"]), "b1c": b1c,
            "w2_t": wT(g["L0_ffn_w2"]), "b2c": b2c,
            "flags": flags, "consts": consts, "onesf": onesf,
            "identb": identb, "identf": identf,
            "mask_lo": mask_lo, "mask_up": mask_up,
            "ones1": ones1, "onesk": onesk,
        }
        maps.append(m)
    return maps


_BUILD_CACHE = {}


def kernel(**inputs):
    L = 4096
    T = L // 2
    if L not in _BUILD_CACHE:
        _BUILD_CACHE[L] = build(L)
    nc, _ = _BUILD_CACHE[L]
    maps = stage_inputs(inputs, L)
    res = run_bass_kernel_spmd(nc, maps, core_ids=list(range(8)))
    out = np.zeros((4, L, D), np.float32)
    for core in range(8):
        b, half = core // 2, core % 2
        out[b, half * T:(half + 1) * T, :] = res.results[core]["out"].T
    return out
